# version 16
# baseline (speedup 1.0000x reference)
import numpy as np
from contextlib import ExitStack

import concourse.bass as bass
import concourse.mybir as mybir
from concourse.bass_utils import run_bass_kernel_spmd

F32 = mybir.dt.float32
BF16 = mybir.dt.bfloat16
AF = mybir.ActivationFunctionType
ALU = mybir.AluOpType
AX = mybir.AxisListType

D = 2048
T = 2048
H = 8192
NTILE = 16
HTILE = 64
TG = 1024
NGRP = T // TG
MMW = 512
EPS = 1e-5
INV_D = 1.0 / D
SCL_G = 0.125
SCL_W2 = float(1.0 / np.sqrt(float(H)))

AG_W1 = 0
AG_W3 = 16384
AG_W2 = 32768
AG_ROWS = 49152


def _hadamard(n):
    m = np.array([[1.0]], np.float32)
    while m.shape[0] < n:
        m = np.block([[m, m], [m, -m]]).astype(np.float32)
    return m


class _Rec:

    NQ = 4

    def __init__(self):
        self.ops = {"sync": [], "scalar": [], "vector": [], "tensor": [],
                    "gpsimd": []}
        self.cnt = {}
        self.fam = {"sync": [f"qL{i}" for i in range(self.NQ)],
                    "gpsimd": [f"qG{i}" for i in range(self.NQ)]}
        self.rot = {"sync": 0, "gpsimd": 0}

    def rec(self, engine, fn, waits=(), inc=None, inc_n=1):
        wl = []
        seen = {}
        for s, v in waits:
            assert v <= self.cnt.get(s, 0), \
                f"wait {s}>={v} not yet producible ({self.cnt.get(s, 0)})"
            if s not in seen or seen[s] < v:
                seen[s] = v
        wl = sorted(seen.items())
        if inc is not None:
            self.cnt[inc] = self.cnt.get(inc, 0) + inc_n
        self.ops[engine].append((tuple(wl), fn, inc, inc_n))

    def dma(self, engine, fn, waits=()):
        fam = self.fam[engine]
        sem = fam[self.rot[engine] % self.NQ]
        self.rot[engine] += 1
        prev = self.cnt.get(sem, 0)
        self.rec(engine, fn, waits=list(waits) + [(sem, prev)])
        self.cnt[sem] = prev + 16
        w, f, _, _ = self.ops[engine].pop()
        self.ops[engine].append((w, f, sem, 16))
        return (sem, prev + 16)

    def fence(self, engine):
        return [(s, self.cnt[s]) for s in self.fam[engine] if s in self.cnt]

    def get(self, sem):
        return self.cnt.get(sem, 0)


def _cover(tokens):
    best = {}
    for s, v in tokens:
        if best.get(s, -1) < v:
            best[s] = v
    return sorted(best.items())


def build_nc():
    nc = bass.Bass("TRN2", target_bir_lowering=False)

    x_in = nc.dram_tensor("x_in", [T, D], F32, kind="ExternalInput")
    xp_in = nc.dram_tensor("xp_in", [T, D], F32, kind="ExternalInput")
    wq_in = nc.dram_tensor("wq_in", [D, D], F32, kind="ExternalInput")
    wk_in = nc.dram_tensor("wk_in", [D, D], F32, kind="ExternalInput")
    wv_in = nc.dram_tensor("wv_in", [D, D], F32, kind="ExternalInput")
    wo_in = nc.dram_tensor("wo_in", [D, D], F32, kind="ExternalInput")
    w1s_in = nc.dram_tensor("w1s_in", [1024, D], F32, kind="ExternalInput")
    w3s_in = nc.dram_tensor("w3s_in", [1024, D], F32, kind="ExternalInput")
    w2s_in = nc.dram_tensor("w2s_in", [256, H], F32, kind="ExternalInput")
    y_out = nc.dram_tensor("y_out", [T, D], F32, kind="ExternalOutput")

    n_sp = nc.dram_tensor("n_sp", [T, D], BF16)
    vt_sp = nc.dram_tensor("vt_sp", [NTILE, D, 128], BF16)
    xn_sp = nc.dram_tensor("xn_sp", [T, D], BF16)
    m_sp = nc.dram_tensor("m_sp", [T, D], BF16)
    ot_sp = nc.dram_tensor("ot_sp", [D, T], BF16)
    w1bf_sp = nc.dram_tensor("w1bf_sp", [1024, D], BF16)
    w3bf_sp = nc.dram_tensor("w3bf_sp", [1024, D], BF16)
    w2bf_sp = nc.dram_tensor("w2bf_sp", [256, H], BF16)
    ccg_o = nc.dram_tensor("ccg_o", [D, D], BF16)
    ag_i = nc.dram_tensor("ag_i", [AG_ROWS, 128], BF16)
    ag_o = nc.dram_tensor("ag_o", [8 * AG_ROWS, 128], BF16,
                          addr_space="Shared")

    R = _Rec()
    rec = R.rec
    dma = R.dma

    with ExitStack() as ctx:
        e = ctx.enter_context
        bufA = e(nc.sbuf_tensor("bufA", [128, 32 * 1024], BF16))
        bufB = e(nc.sbuf_tensor("bufB", [128, 32 * 1024], BF16))
        bufC = e(nc.sbuf_tensor("bufC", [128, 16 * 1024], BF16))
        w13p = [e(nc.sbuf_tensor("w13p0", [128, 4096], BF16)),
                e(nc.sbuf_tensor("w13p1", [128, 4096], BF16))]
        xstg = e(nc.sbuf_tensor("xstg", [128, D], F32))
        fstg = e(nc.sbuf_tensor("fstg", [128, D], F32))
        wstg = e(nc.sbuf_tensor("wstg", [128, D], F32))
        bstg = e(nc.sbuf_tensor("bstg", [128, D], BF16))
        stat = e(nc.sbuf_tensor("stat", [128, 160], F32))
        psumA = e(nc.psum_tensor("psumA", [128, D], F32))
        psumB = e(nc.psum_tensor("psumB", [128, D], F32))

        sems = {}
        for nm in (R.fam["sync"] + R.fam["gpsimd"]
                   + ["c_ar", "c_ag", "sT", "sV", "sS"]):
            sems[nm] = e(nc.semaphore(nm))

        PS = (psumA, psumB)

        def tile2k(buf, t):
            return buf[:, t * D:(t + 1) * D]

        def hid_tile(a):
            buf = bufA if a < 32 else bufB
            return buf[:, (a % 32) * TG:((a % 32) + 1) * TG]

        def ln_stats(x_ap, sb, waits_extra, sq_out=None):
            b24 = sb * 24
            for c in range(4):
                rec("vector", (lambda c=c, b24=b24: nc.vector.bn_stats(
                    stat[:, b24 + 6 * c:b24 + 6 * (c + 1)],
                    x_ap[:, 512 * c:512 * (c + 1)])),
                    waits=(waits_extra if c == 0 else []), inc="sV")
            rec("vector", (lambda b24=b24, sb=sb: nc.vector.bn_aggr(
                stat[:, 96 + 2 * sb:98 + 2 * sb], stat[:, b24:b24 + 24])),
                waits=[("sV", R.get("sV"))], inc="sV")
            v_var = R.get("sV")
            rec("scalar", (lambda sb=sb: nc.scalar.activation(
                stat[:, 97 + 2 * sb:98 + 2 * sb],
                stat[:, 97 + 2 * sb:98 + 2 * sb], AF.Ln,
                bias=stat[:, 105:106])),
                waits=[("sV", v_var)], inc="sS")
            rec("scalar", (lambda sb=sb: nc.scalar.activation(
                stat[:, 97 + 2 * sb:98 + 2 * sb],
                stat[:, 97 + 2 * sb:98 + 2 * sb], AF.Exp,
                scale=stat[:, 106:107])),
                waits=[("sS", R.get("sS"))], inc="sS")
            return R.get("sS")

        def ln_norm(x_ap, nrm_out, sb, v_rstd, store_fn, extra_w=()):
            w = [("sS", v_rstd)] + list(extra_w)
            rec("vector", (lambda sb=sb: nc.vector.tensor_scalar(
                nrm_out, x_ap, stat[:, 96 + 2 * sb:97 + 2 * sb],
                stat[:, 97 + 2 * sb:98 + 2 * sb],
                op0=ALU.subtract, op1=ALU.mult)), waits=w, inc="sV")
            nv = R.get("sV")
            tok = None
            if store_fn is not None:
                tok = dma("gpsimd", store_fn, waits=[("sV", nv)])
            return nv, tok

        rec("vector", lambda: nc.vector.memset(stat[:, 105:106], EPS), inc="sV")
        rec("vector", lambda: nc.vector.memset(stat[:, 106:107], -0.5), inc="sV")
        rec("vector", lambda: nc.vector.memset(stat[:, 104:105], SCL_W2),
            inc="sV")

        ln_norm_v = {}
        n_spill_t = []

        w2buf = [bufC[:, 0:H], bufC[:, H:2 * H]]
        pong = bufB[:, 0:H]
        w2_cast_t = []
        for tl in range(2):
            w2_cast_t.append(dma("gpsimd", (lambda tl=tl: nc.gpsimd.dma_start(
                out=w2buf[tl], in_=w2s_in[tl * 128:(tl + 1) * 128, :]))))


        def ln_pipe(items, norm_map, spill, init_gates=()):
            pend = {}
            pend_done = {}
            for k in range(len(items) + 1):
                if k < len(items):
                    src_t, i, dstt, gx = items[k]
                    xb = (xstg, wstg, fstg)[k % 3]
                    w = []
                    if k >= 3:
                        w.append(("sV", pend_done[k - 3]))
                    elif init_gates:
                        w.append(init_gates[k])
                    t_x = dma("sync", (lambda src_t=src_t, i=i, xb=xb:
                        nc.sync.dma_start(
                            out=xb[:, :],
                            in_=src_t[i * 128:(i + 1) * 128, :])), waits=w)
                    sb = k % 3
                    rstd = ln_stats(xb[:, :], sb, [t_x])
                    pend[k] = (i, xb, sb, rstd, dstt, gx)
                j = k - 1
                if j >= 0:
                    i, xb, sb, rstd, dstt, gx = pend.pop(j)
                    if spill:
                        sf = (lambda i=i, dstt=dstt: nc.gpsimd.dma_start(
                            out=n_sp[i * 128:(i + 1) * 128, :], in_=dstt))
                    else:
                        sf = None
                    nv, tok = ln_norm(xb[:, :], dstt, sb, rstd, sf, gx)
                    norm_map[i] = nv
                    pend_done[j] = nv
                    if spill:
                        n_spill_t.append(tok)

        ln_pipe([(x_in, i, tile2k(bufA, i), []) for i in range(NTILE)],
                ln_norm_v, True)
        ln1_own_v = ln_norm_v[NTILE - 1]

        prep_bf_t = {}
        for name, src_, dst in (("w1", w1s_in, w1bf_sp), ("w3", w3s_in, w3bf_sp)):
            toks = []
            prev_store = None
            for i in range(4):
                w = [prev_store] if prev_store is not None else []
                t_l = dma("gpsimd", (lambda src_=src_, i=i: nc.gpsimd.dma_start(
                    out=w13p[0][:, 0:4096].rearrange("p (c j) -> p c j", c=2),
                    in_=src_.ap().rearrange("(c p) e -> p c e", p=128)
                        [:, 2 * i:2 * i + 2, :])), waits=w)
                prev_store = dma("gpsimd", (lambda dst=dst, i=i: nc.gpsimd.dma_start(
                    out=dst[i * 256:(i + 1) * 256, :].rearrange(
                        "(c p) e -> p c e", p=128),
                    in_=w13p[0][:, 0:4096].rearrange("p (c j) -> p c j", c=2))),
                    waits=[t_l])
                toks.append(prev_store)
            prep_bf_t[name] = toks

        bf_tasks = []
        w2_done_v = [None, None]

        def mk_bf(tl, half, op, g, bs, cur, oth, wv_):
            def emit():
                def bf():
                    vc = cur.rearrange("p (g two b) -> p g two b",
                                       g=g, two=2, b=bs)
                    vo = oth.rearrange("p (g two b) -> p g two b",
                                       g=g, two=2, b=bs)
                    return nc.vector.tensor_tensor(
                        vo[:, :, half, :], vc[:, :, 0, :], vc[:, :, 1, :], op)
                rec("vector", bf, waits=wv_, inc="sV")
            return emit

        for tl in range(2):
            cur, oth = w2buf[tl], pong
            first = True
            dists = [32 * 128, 16 * 128, 8 * 128, 4 * 128, 2 * 128, 128,
                     64, 32, 16, 8, 4, 2, 1]
            for bs in dists:
                g_ = H // (2 * bs)
                wv_ = [w2_cast_t[tl]] if first else []
                first = False
                bf_tasks.append(mk_bf(tl, 0, ALU.add, g_, bs, cur, oth, wv_))
                bf_tasks.append(mk_bf(tl, 1, ALU.subtract, g_, bs, cur, oth, []))
                cur, oth = oth, cur
            assert cur is pong

            def mk_copy(tl=tl):
                def emit():
                    rec("vector", (lambda tl=tl: nc.vector.tensor_copy(
                        w2buf[tl], pong)), inc="sV")
                    w2_done_v[tl] = R.get("sV")
                return emit
            bf_tasks.append(mk_copy())

        agin_t = []
        prepT13 = []

        prepT_n = [0]

        def mk_prepT13(name, bfsp, base, dtile):
            def emit():
                k = prepT_n[0]; prepT_n[0] += 1
                stg = w13p[1][:, (k % 2) * 1024:(k % 2) * 1024 + 1024]
                w = _cover(prep_bf_t[name]) if dtile == 0 else []
                if len(agin_t) >= 2:
                    w = _cover([agin_t[-2]]) + w
                t_l = dma("sync", (lambda stg=stg: nc.sync.dma_start(
                    out=stg,
                    in_=bfsp[0:1024, dtile * 128:(dtile + 1) * 128],
                    transpose=True)), waits=w)
                agin_t.append(dma("gpsimd", (lambda stg=stg: nc.gpsimd.dma_start(
                    out=ag_i.ap().rearrange("(ht r) c -> ht r c", r=2048)
                        [base // 2048:base // 2048 + 8,
                         dtile * 128:(dtile + 1) * 128, :].rearrange(
                        "ht p c -> p ht c"),
                    in_=stg.rearrange("p (ht c) -> p ht c", ht=8))),
                    waits=[t_l]))
            return emit

        for name, bfsp, base in (("w1", w1bf_sp, AG_W1), ("w3", w3bf_sp, AG_W3)):
            for dtile in range(NTILE):
                prepT13.append(mk_prepT13(name, bfsp, base, dtile))

        part_order = list(range(4, NTILE)) + [0, 1, 2, 3]
        ln_part_v = {}
        g_evac_v = {}
        g_store_t = []
        pln_pend = {}
        pln_done = {}

        def pln_stats(s):
            i = part_order[s]
            xb = (xstg, wstg)[s % 2]
            w = []
            if s >= 2:
                w.append(("sV", pln_done[s - 2]))
            else:
                w.append(("sV", ln_norm_v[(15, 13)[s]]))
            t_x = dma("sync", (lambda i=i, xb=xb: nc.sync.dma_start(
                out=xb[:, :], in_=xp_in[i * 128:(i + 1) * 128, :])), waits=w)
            sb = s % 2
            rstd = ln_stats(xb[:, :], sb, [t_x])
            pln_pend[s] = (i, xb, sb, rstd)

        def pln_norm(s):
            i, xb, sb, rstd = pln_pend.pop(s)
            ew = [("sV", w2_done_v[1])] if i < 4 else []
            nv, _ = ln_norm(xb[:, :], tile2k(bufB, i), sb, rstd, None, ew)
            ln_part_v[i] = nv
            pln_done[s] = nv

        bq = list(bf_tasks)
        for dt in range(NTILE):
            if dt >= 2:
                for _ in range(5):
                    if bq:
                        bq.pop(0)()
            if dt == 12:
                while bq:
                    bq.pop(0)()
            pln_stats(dt)
            if dt >= 1:
                pln_norm(dt - 1)
            if dt == NTILE - 1:
                pln_norm(dt)
            if dt >= 8:
                for _ in range(4):
                    if prepT13:
                        prepT13.pop(0)()
            ps = PS[dt % 2]
            for tc in range(NTILE):
                w0 = []
                if dt == 0:
                    w0.append(("sV", ln_norm_v[tc]))
                if dt >= 2 and tc == 0:
                    w0.append(("sV", g_evac_v[dt - 2]))
                for q in range(4):
                    is_last = (tc == NTILE - 1 and q == 3)
                    rec("tensor", (lambda dt=dt, tc=tc, q=q, ps=ps:
                        nc.tensor.matmul(
                            ps[:, q * MMW:(q + 1) * MMW],
                            tile2k(bufA, tc)[:, dt * 128:(dt + 1) * 128],
                            tile2k(bufA, tc)[:, q * MMW:(q + 1) * MMW],
                            start=(tc == 0), stop=(tc == NTILE - 1))),
                        waits=(w0 if q == 0 else []),
                        inc=("sT" if is_last else None))
            vt_ = R.get("sT")
            w = [("sT", vt_)]
            if g_store_t:
                w.append(g_store_t[-1])
            rec("vector", (lambda ps=ps: nc.vector.tensor_scalar_mul(
                bstg[:, :], ps[:, :], SCL_G)), waits=w, inc="sV")
            g_evac_v[dt] = R.get("sV")
            g_store_t.append(dma("sync", (lambda dt=dt: nc.sync.dma_start(
                out=ccg_o[dt * 128:(dt + 1) * 128, :], in_=bstg[:, :])),
                waits=[("sV", g_evac_v[dt])]))
        assert not bq
        g_own_last_grp = R.get("sT")

        w2sp_t = []
        for tl in range(2):
            w2sp_t.append(dma("gpsimd", (lambda tl=tl: nc.gpsimd.dma_start(
                out=w2bf_sp[tl * 128:(tl + 1) * 128, :], in_=w2buf[tl])),
                waits=[("sV", w2_done_v[tl])]))
        prep_bf_t["w2"] = w2sp_t

        def mk_prepT2(hc):
            def emit():
                stg = w13p[1][:, (hc % 2) * 1024:(hc % 2) * 1024 + 256]
                w = _cover(prep_bf_t["w2"]) if hc == 0 else []
                if len(agin_t) >= 2:
                    w += _cover([agin_t[-2]])
                t_l = dma("sync", (lambda stg=stg: nc.sync.dma_start(
                    out=stg,
                    in_=w2bf_sp[0:256, hc * 128:(hc + 1) * 128],
                    transpose=True)), waits=w)
                agin_t.append(dma("gpsimd", (lambda stg=stg: nc.gpsimd.dma_start(
                    out=ag_i.ap().rearrange("(gp r) c -> gp r c", r=8192)
                        [AG_W2 // 8192:AG_W2 // 8192 + 2,
                         hc * 128:(hc + 1) * 128, :].rearrange(
                        "gp p c -> p gp c"),
                    in_=stg.rearrange("p (gp c) -> p gp c", gp=2))),
                    waits=[t_l]))
            return emit

        prepT2 = [mk_prepT2(hc) for hc in range(HTILE)]

        g2_evac_v = {}
        ownld_t = {}
        for dt in range(NTILE):
            ps = PS[dt % 2]
            w = [g_store_t[dt]]
            if dt >= 1:
                w.append(("sV", g2_evac_v[dt - 1]))
            else:
                w += _cover(g_store_t)
            ownld_t[dt] = dma("sync", (lambda dt=dt: nc.sync.dma_start(
                out=bstg[:, :], in_=ccg_o[dt * 128:(dt + 1) * 128, :])),
                waits=w)
            for tcx in range(NTILE):
                tc = part_order[tcx]
                w0 = []
                if dt == 0:
                    w0.append(("sV", ln_part_v[tc]))
                if dt < 2 and tcx == 0:
                    w0.append(("sV", g_evac_v[NTILE - 2 + dt]))
                if dt >= 2 and tcx == 0:
                    w0.append(("sV", g2_evac_v[dt - 2]))
                for q in range(4):
                    is_last = (tcx == NTILE - 1 and q == 3)
                    rec("tensor", (lambda dt=dt, tc=tc, q=q, ps=ps, tcx=tcx:
                        nc.tensor.matmul(
                            ps[:, q * MMW:(q + 1) * MMW],
                            tile2k(bufB, tc)[:, dt * 128:(dt + 1) * 128],
                            tile2k(bufB, tc)[:, q * MMW:(q + 1) * MMW],
                            start=(tcx == 0), stop=(tcx == NTILE - 1))),
                        waits=(w0 if q == 0 else []),
                        inc=("sT" if is_last else None))
            vt_ = R.get("sT")
            rec("vector", (lambda dt=dt, ps=ps: nc.vector.scalar_tensor_tensor(
                tile2k(bufA, dt), ps[:, :], SCL_G, bstg[:, :],
                op0=ALU.mult, op1=ALU.add)),
                waits=[("sT", vt_), ownld_t[dt]], inc="sV")
            g2_evac_v[dt] = R.get("sV")
        g_last_grp = R.get("sT")
        g_done_v = g2_evac_v[NTILE - 1]
        assert not prepT13

        def nt_src(dc):
            if dc < 8:
                return bufC[:, dc * D:(dc + 1) * D]
            return bufB[:, (dc - 8) * D:(dc - 7) * D]

        nt_t = []
        for r in range(NTILE):
            w = []
            if r == 0:
                w = _cover(n_spill_t + w2sp_t)
            elif r == 8:
                w = [("sT", g_last_grp)]
            nt_t.append(dma("sync", (lambda r=r: nc.sync.dma_start(
                out=nt_src(r),
                in_=n_sp[0:T, r * 128:(r + 1) * 128], transpose=True)),
                waits=w))
        nt_cov = _cover(nt_t)

        vt_grp_t = {}
        vt_evac_v = {}
        vt_store_t = []
        wcp_s = {}

        def wstage_load(k, in_fn, waits_extra):
            stg = (xstg, wstg)[k % 2]
            w = list(waits_extra)
            if k >= 2:
                w.append(("sS", wcp_s[k - 2]))
            return dma("sync", (lambda stg=stg, in_fn=in_fn:
                nc.sync.dma_start(
                    out=stg[:, :].rearrange("p (c j) -> p c j", c=16),
                    in_=in_fn())), waits=w), stg

        def wstage_copy(k, t_l, stg, dst, waits_extra):
            rec("scalar", (lambda dst=dst, stg=stg: nc.scalar.copy(
                dst, stg[:, :])), waits=[t_l] + list(waits_extra), inc="sS")
            wcp_s[k] = R.get("sS")
            return wcp_s[k]

        for ft in range(NTILE + 1):
            for _ in range(4):
                if prepT2:
                    prepT2.pop(0)()
            if ft < NTILE:
                pan = w13p[0][:, (ft % 2) * D:((ft % 2) + 1) * D]
                wl = []
                if ft < 2:
                    wl.append(("sV", pln_done[14 + ft]))
                t_l, stg = wstage_load(ft, (lambda ft=ft:
                    wv_in.ap().rearrange("(c p) e -> p c e", p=128)
                        [:, :, ft * 128:(ft + 1) * 128]), wl)
                wc = []
                if ft >= 2:
                    wc.append(("sT", vt_grp_t[ft - 2]))
                if ft == 0:
                    wc += _cover(prep_bf_t["w1"] + prep_bf_t["w3"])
                wstage_copy(ft, t_l, stg, pan, wc)
            if ft >= 1:
                f = ft - 1
                pan = w13p[0][:, (f % 2) * D:((f % 2) + 1) * D]
                ps = PS[f % 2]
                w0 = [("sS", wcp_s[f])] + nt_cov
                if f == 0:
                    w0.append(("sV", g2_evac_v[NTILE - 2]))
                elif f == 1:
                    w0.append(("sV", g2_evac_v[NTILE - 1]))
                else:
                    w0.append(("sV", vt_evac_v[f - 2]))
                for dc in range(NTILE):
                    for q in range(4):
                        is_last = (dc == NTILE - 1 and q == 3)
                        rec("tensor", (lambda dc=dc, q=q, ps=ps, pan=pan:
                            nc.tensor.matmul(
                                ps[:, q * MMW:(q + 1) * MMW],
                                pan[:, dc * 128:(dc + 1) * 128],
                                nt_src(dc)[:, q * MMW:(q + 1) * MMW],
                                start=(dc == 0), stop=(dc == NTILE - 1))),
                            waits=(w0 if (dc == 0 and q == 0) else []),
                            inc=("sT" if is_last else None))
                vt_grp_t[f] = R.get("sT")
                w = [("sT", vt_grp_t[f])]
                if vt_store_t:
                    w.append(vt_store_t[-1])
                rec("vector", (lambda ps=ps: nc.vector.tensor_copy(
                    bstg[:, :], ps[:, :])), waits=w, inc="sV")
                vt_evac_v[f] = R.get("sV")
                vt_store_t.append(dma("gpsimd", (lambda f=f: nc.gpsimd.dma_start(
                    out=vt_sp.ap()[:, f * 128:(f + 1) * 128, :].rearrange(
                        "tt p j -> p tt j"),
                    in_=bstg[:, :].rearrange("p (tt j) -> p tt j", tt=16))),
                    waits=[("sV", vt_evac_v[f])]))
        vt_all_evac_v = vt_evac_v[NTILE - 1]
        vt_last_grp = vt_grp_t[NTILE - 1]
        assert not prepT2
        rec("gpsimd", (lambda: nc.gpsimd.collective_compute(
            "AllGather", ALU.bypass,
            replica_groups=[[0, 1, 2, 3, 4, 5, 6, 7]],
            ins=[ag_i.ap().opt()], outs=[ag_o.ap().opt()])),
            waits=_cover(agin_t), inc="c_ag")
        ag_done = R.get("c_ag")

        t2_evac_s = {}
        t2_qgrp = {}
        wk_q_s = {}

        def quarter_unit(kbase, q, cc, w_in_t, dest_half_gate, out_s):
            k = kbase + 4 * q + cc
            t_l, stg = wstage_load(k, (lambda q=q, cc=cc, w_in_t=w_in_t:
                w_in_t.ap().rearrange("(c p) e -> p c e", p=128)
                    [:, 4 * cc:4 * cc + 4, q * MMW:(q + 1) * MMW]), [])
            dst = bufC[:, (q % 2) * H + cc * 2048:(q % 2) * H + cc * 2048
                       + 2048]
            out_s[q] = wstage_copy(k, t_l, stg, dst, dest_half_gate)

        for cc in range(4):
            quarter_unit(16, 0, cc, wk_in, [("sT", vt_last_grp)], wk_q_s)
        kk = 0
        for q in range(4):
            for dt in range(NTILE):
                if q < 3 and dt in (2, 5, 8, 11):
                    g = ([("sT", vt_last_grp)] if q == 0
                         else [("sT", t2_qgrp[q - 1])])
                    quarter_unit(16, q + 1, (dt - 2) // 3, wk_in, g, wk_q_s)
                bank = kk % 8
                ps = PS[bank // 4][:, (bank % 4) * MMW:((bank % 4) + 1) * MMW]
                w0 = [("sS", wk_q_s[q])]
                if kk == 0:
                    w0.append(("sV", g_done_v))
                    w0.append(("sV", vt_all_evac_v))
                if kk >= 8:
                    w0.append(("sS", t2_evac_s[kk - 8]))
                for dc in range(NTILE):
                    is_last = (dc == NTILE - 1)
                    rec("tensor", (lambda dt=dt, dc=dc, q=q, ps=ps:
                        nc.tensor.matmul(
                            ps,
                            tile2k(bufA, dc)[:, dt * 128:(dt + 1) * 128],
                            bufC[:, (q % 2) * H + dc * MMW:
                                 (q % 2) * H + (dc + 1) * MMW],
                            start=(dc == 0), stop=is_last)),
                        waits=(w0 if dc == 0 else []),
                        inc=("sT" if is_last else None))
                vt_ = R.get("sT")
                rec("scalar", (lambda dt=dt, q=q, ps=ps: nc.scalar.copy(
                    tile2k(bufB, dt)[:, q * MMW:(q + 1) * MMW], ps)),
                    waits=[("sT", vt_)], inc="sS")
                t2_evac_s[kk] = R.get("sS")
                kk += 1
            t2_qgrp[q] = R.get("sT")
        t2_done_s = R.get("sS")

        pq_pan_s = {}
        p_grp_t = {}
        pa_v = {}
        for et in range(NTILE + 1):
            if et < NTILE:
                pa = w13p[et % 2][:, 0:2048]
                t_l, stg = wstage_load(32 + et, (lambda et=et:
                    wq_in.ap().rearrange("(c p) e -> p c e", p=128)
                        [:, :, et * 128:(et + 1) * 128]), [])
                wc = []
                if et >= 2:
                    wc.append(("sT", p_grp_t[et - 2]))
                if et == 0:
                    wc.append(("sT", vt_grp_t[14]))
                if et == 1:
                    wc += _cover(agin_t)
                pq_pan_s[et] = wstage_copy(32 + et, t_l, stg, pa, wc)
            if et >= 1:
                f = et - 1
                pa = w13p[f % 2][:, 0:2048]
                ps = PS[f % 2]
                w0 = [("sS", pq_pan_s[f]), ("sS", t2_done_s)]
                if f >= 2:
                    w0.append(("sV", pa_v[f - 2]))
                for dc in range(NTILE):
                    for q in range(4):
                        is_last = (dc == NTILE - 1 and q == 3)
                        rec("tensor", (lambda dc=dc, q=q, ps=ps, pa=pa:
                            nc.tensor.matmul(
                                ps[:, q * MMW:(q + 1) * MMW],
                                pa[:, dc * 128:(dc + 1) * 128],
                                tile2k(bufB, dc)[:, q * MMW:(q + 1) * MMW],
                                start=(dc == 0), stop=(dc == NTILE - 1))),
                            waits=(w0 if (dc == 0 and q == 0) else []),
                            inc=("sT" if is_last else None))
                p_grp_t[f] = R.get("sT")
                rec("vector", (lambda ps=ps: nc.vector.reduce_max(
                    stat[:, 0:1], ps[:, :], axis=AX.X, negate=True)),
                    waits=[("sT", p_grp_t[f])], inc="sV")
                vm = R.get("sV")
                rec("scalar", (lambda ps=ps: nc.scalar.activation(
                    fstg[:, :], ps[:, :], AF.Exp, bias=stat[:, 0:1],
                    accum_out=stat[:, 1:2])),
                    waits=[("sV", vm)], inc="sS")
                vs = R.get("sS")
                rec("vector", (lambda: nc.vector.reciprocal(
                    stat[:, 2:3], stat[:, 1:2])),
                    waits=[("sS", vs)], inc="sV")
                vr = R.get("sV")
                rec("vector", (lambda f=f: nc.vector.tensor_scalar_mul(
                    tile2k(bufA, f), fstg[:, :], stat[:, 2:3])),
                    waits=[("sV", vr)], inc="sV")
                pa_v[f] = R.get("sV")
        a_done_v = R.get("sV")

        s_evac_s = {}
        s_qgrp = {}
        wo_q_s = {}
        for cc in range(4):
            quarter_unit(48, 0, cc, wo_in, [("sT", t2_qgrp[2])], wo_q_s)
        kk = 0
        for q in range(4):
            for ftl in range(NTILE):
                if q < 3 and ftl in (2, 5, 8, 11):
                    g = ([("sT", t2_qgrp[3])] if q == 0
                         else [("sT", s_qgrp[q - 1])])
                    quarter_unit(48, q + 1, (ftl - 2) // 3, wo_in, g, wo_q_s)
                bank = kk % 8
                ps = PS[bank // 4][:, (bank % 4) * MMW:((bank % 4) + 1) * MMW]
                w0 = [("sS", wo_q_s[q]), ("sV", a_done_v)]
                if kk >= 8:
                    w0.append(("sS", s_evac_s[kk - 8]))
                for ec in range(NTILE):
                    is_last = (ec == NTILE - 1)
                    rec("tensor", (lambda ftl=ftl, ec=ec, q=q, ps=ps:
                        nc.tensor.matmul(
                            ps,
                            tile2k(bufA, ec)[:, ftl * 128:(ftl + 1) * 128],
                            bufC[:, (q % 2) * H + ec * MMW:
                                 (q % 2) * H + (ec + 1) * MMW],
                            start=(ec == 0), stop=is_last)),
                        waits=(w0 if ec == 0 else []),
                        inc=("sT" if is_last else None))
                vt_ = R.get("sT")
                rec("scalar", (lambda ftl=ftl, q=q, ps=ps: nc.scalar.copy(
                    tile2k(bufB, ftl)[:, q * MMW:(q + 1) * MMW], ps)),
                    waits=[("sT", vt_)], inc="sS")
                s_evac_s[kk] = R.get("sS")
                kk += 1
            s_qgrp[q] = R.get("sT")
        s_done_s = R.get("sS")
        s_last_grp = s_qgrp[3]

        fin_v = {}
        fin_grp = {}
        xn_store_t = []
        ln2_norm_v = {}
        ln2_pend = {}
        m_store_t = []
        for tt in range(NTILE):
            vp = bufC[:, (tt % 3) * D:((tt % 3) + 1) * D]
            xb = (xstg, wstg)[tt % 2]
            w = []
            if tt == 0:
                w = _cover(vt_store_t) + [("sS", s_done_s)]
            if tt >= 3:
                w.append(("sT", fin_grp[tt - 3]))
            t_p = dma("sync", (lambda tt=tt, vp=vp: nc.sync.dma_start(
                out=vp.rearrange("p (c j) -> p c j", c=16),
                in_=vt_sp.ap()[tt, :, :].rearrange("(c p) j -> p c j",
                                                   p=128))), waits=w)
            wx = [("sV", fin_v[tt - 2])] if tt >= 2 else \
                [("sS", wcp_s[62 + tt])]
            t_x = dma("sync", (lambda tt=tt, xb=xb: nc.sync.dma_start(
                out=xb[:, :], in_=x_in[tt * 128:(tt + 1) * 128, :])),
                waits=wx)
            ps = PS[tt % 2]
            w0 = [t_p, t_x, ("sS", s_done_s)]
            if tt >= 2:
                w0.append(("sV", fin_v[tt - 2]))
            for fc in range(NTILE):
                for q in range(4):
                    is_last = (fc == NTILE - 1 and q == 3)
                    rec("tensor", (lambda fc=fc, q=q, ps=ps, vp=vp:
                        nc.tensor.matmul(
                            ps[:, q * MMW:(q + 1) * MMW],
                            vp[:, fc * 128:(fc + 1) * 128],
                            tile2k(bufB, fc)[:, q * MMW:(q + 1) * MMW],
                            start=(fc == 0), stop=(fc == NTILE - 1))),
                        waits=(w0 if (fc == 0 and q == 0) else []),
                        inc=("sT" if is_last else None))
            fin_grp[tt] = R.get("sT")
            xb2 = tile2k(bufA, tt % 2)
            ob2 = tile2k(bufA, 2 + tt % 2)
            w = [("sT", fin_grp[tt]), t_x]
            if tt == 0:
                w.append(("sT", s_last_grp))
            if tt >= 2:
                w.append(("sV", ln2_norm_v[tt - 2]))
                w.append(xn_store_t[tt - 2])
            rec("vector", (lambda ps=ps, xb=xb, xb2=xb2: nc.vector.tensor_add(
                xb2, xb[:, :], ps[:, :])), waits=w, inc="sV")
            fin_v[tt] = R.get("sV")
            xn_store_t.append(dma("gpsimd", (lambda tt=tt, xb2=xb2:
                nc.gpsimd.dma_start(
                    out=xn_sp[tt * 128:(tt + 1) * 128, :], in_=xb2)),
                waits=[("sV", fin_v[tt])]))

            sb = tt % 2
            rstd = ln_stats(xb2, sb, [("sV", fin_v[tt])])
            ln2_pend[tt] = (xb2, ob2, sb, rstd)

            def ln2_emit(j):
                xbj, obj, sbj, rstdj = ln2_pend.pop(j)
                ew = [m_store_t[-1]] if m_store_t else []
                nv, tok = ln_norm(
                    xbj, obj, sbj, rstdj,
                    (lambda j=j, obj=obj: nc.gpsimd.dma_start(
                        out=m_sp[j * 128:(j + 1) * 128, :], in_=obj)), ew)
                ln2_norm_v[j] = nv
                m_store_t.append(tok)
            if tt >= 1:
                ln2_emit(tt - 1)
            if tt == NTILE - 1:
                ln2_emit(tt)
        ln2_all_v = ln2_norm_v[NTILE - 1]

        hid_mm_t = {}
        hid_v = {}
        ot_mm_t = {}
        ot_s = {}
        ot_store_t = []
        for g in range(NGRP):
            mt_t = []
            for r in range(NTILE):
                w = []
                if g == 0:
                    if r == 0:
                        w += _cover(m_store_t[:8])
                        w.append(("sT", fin_grp[NTILE - 1]))
                else:
                    if r == 0:
                        w.append(("sT", ot_mm_t[(g - 1, NTILE - 2)]))
                    elif r == 8:
                        w.append(("sT", ot_mm_t[(g - 1, NTILE - 1)]))
                mt_t.append(dma("sync", (lambda g=g, r=r: nc.sync.dma_start(
                    out=bufC[:, r * TG:(r + 1) * TG],
                    in_=m_sp[g * TG + 0:(g + 1) * TG,
                             r * 128:(r + 1) * 128], transpose=True)),
                    waits=w))
            mt_cov = _cover(mt_t)

            for a in range(HTILE):
                rk, loc = a // 8, a % 8
                pa = w13p[a % 2]
                w = []
                if g == 0 and a < 2:
                    w.append(("c_ag", ag_done))
                    w.append(("sV", ln2_all_v))
                    w += _cover(m_store_t)
                if a >= 2:
                    w.append(("sT", hid_mm_t[(g, a - 2)]))
                base1 = (rk * AG_ROWS + AG_W1 + loc * 2048) // 128
                base3 = (rk * AG_ROWS + AG_W3 + loc * 2048) // 128
                t1 = dma("sync", (lambda base1=base1, pa=pa: nc.sync.dma_start(
                    out=pa[:, 0:2048].rearrange("p (c j) -> p c j", c=16),
                    in_=ag_o.ap().rearrange("(r p) c -> r p c", p=128)
                        [base1:base1 + 16].rearrange("r p c -> p r c"))),
                    waits=w)
                t3 = dma("sync", (lambda base3=base3, pa=pa: nc.sync.dma_start(
                    out=pa[:, 2048:4096].rearrange("p (c j) -> p c j", c=16),
                    in_=ag_o.ap().rearrange("(r p) c -> r p c", p=128)
                        [base3:base3 + 16].rearrange("r p c -> p r c"))))
                ps = PS[a % 2]
                w0 = [t1, t3] + mt_cov
                if g == 0 and a < 2:
                    w0.append(("sV", fin_v[14 + a]))
                elif a < 2:
                    w0.append(("sS", ot_s[(g - 1, 14 + a)]))
                else:
                    w0.append(("sV", hid_v[(g, a - 2)]))
                for dc in range(NTILE):
                    for wsel in range(2):
                        for hhf in range(2):
                            is_last = (dc == NTILE - 1 and wsel == 1
                                       and hhf == 1)
                            rec("tensor", (lambda dc=dc, wsel=wsel, hhf=hhf,
                                           ps=ps, pa=pa:
                                nc.tensor.matmul(
                                    ps[:, wsel * TG + hhf * MMW:
                                       wsel * TG + (hhf + 1) * MMW],
                                    pa[:, wsel * 2048 + dc * 128:
                                       wsel * 2048 + (dc + 1) * 128],
                                    bufC[:, dc * TG + hhf * MMW:
                                         dc * TG + (hhf + 1) * MMW],
                                    start=(dc == 0), stop=(dc == NTILE - 1))),
                                waits=(w0 if (dc == 0 and wsel == 0
                                              and hhf == 0) else []),
                                inc=("sT" if is_last else None))
                hid_mm_t[(g, a)] = R.get("sT")
                rec("scalar", (lambda ps=ps: nc.scalar.activation(
                    fstg[:, 0:TG], ps[:, 0:TG], AF.Silu)),
                    waits=[("sT", hid_mm_t[(g, a)])], inc="sS")
                vs = R.get("sS")
                rec("vector", (lambda a=a, ps=ps: nc.vector.tensor_mul(
                    hid_tile(a), fstg[:, 0:TG], ps[:, TG:2 * TG])),
                    waits=[("sS", vs)], inc="sV")
                hid_v[(g, a)] = R.get("sV")
            hid_all_v = R.get("sV")

            for gt in range(NTILE):
                rk2, loc2 = gt // 2, gt % 2
                base2 = (rk2 * AG_ROWS + AG_W2 + loc2 * 8192) // 128
                pa = bufC[:, (gt % 2) * H:((gt % 2) + 1) * H]
                w = []
                if gt < 2:
                    w.append(("sT", hid_mm_t[(g, HTILE - 1)]))
                else:
                    w.append(("sT", ot_mm_t[(g, gt - 2)]))
                t_l = dma("sync", (lambda base2=base2, pa=pa: nc.sync.dma_start(
                    out=pa.rearrange("p (c j) -> p c j", c=64),
                    in_=ag_o.ap().rearrange("(r p) c -> r p c", p=128)
                        [base2:base2 + 64].rearrange("r p c -> p r c"))),
                    waits=w)
                ps = PS[gt % 2]
                w0 = [t_l, ("sV", hid_all_v)]
                if gt >= 2:
                    w0.append(("sS", ot_s[(g, gt - 2)]))
                for hc in range(HTILE):
                    for hhf in range(2):
                        is_last = (hc == HTILE - 1 and hhf == 1)
                        rec("tensor", (lambda hc=hc, hhf=hhf, ps=ps, pa=pa:
                            nc.tensor.matmul(
                                ps[:, hhf * MMW:(hhf + 1) * MMW],
                                pa[:, hc * 128:(hc + 1) * 128],
                                hid_tile(hc)[:, hhf * MMW:(hhf + 1) * MMW],
                                start=(hc == 0), stop=(hc == HTILE - 1))),
                            waits=(w0 if (hc == 0 and hhf == 0) else []),
                            inc=("sT" if is_last else None))
                ot_mm_t[(g, gt)] = R.get("sT")
                w = [("sT", ot_mm_t[(g, gt)])]
                if ot_store_t:
                    w.append(ot_store_t[-1])
                rec("scalar", (lambda ps=ps: nc.scalar.activation(
                    bstg[:, 0:TG], ps[:, 0:TG], AF.Copy,
                    scale=stat[:, 104:105])), waits=w, inc="sS")
                ot_s[(g, gt)] = R.get("sS")
                ot_store_t.append(dma("gpsimd", (lambda gt=gt, g=g:
                    nc.gpsimd.dma_start(
                        out=ot_sp[gt * 128:(gt + 1) * 128,
                                  g * TG:(g + 1) * TG],
                        in_=bstg[:, 0:TG])),
                    waits=[("sS", ot_s[(g, gt)])]))

        out_v = {}
        y_t = []
        for tt in range(NTILE):
            ob = w13p[tt % 2][:, 0:D]
            xb = w13p[tt % 2][:, D:2 * D]
            w = []
            if tt == 0:
                w += _cover(ot_store_t[:NTILE])
                w.append(("sT", hid_mm_t[(NGRP - 1, HTILE - 1)]))
            elif tt == 8:
                w += _cover(ot_store_t)
            if tt >= 2:
                w.append(("sV", out_v[tt - 2]))
            t_o = dma("sync", (lambda tt=tt, ob=ob: nc.sync.dma_start(
                out=ob,
                in_=ot_sp[0:D, tt * 128:(tt + 1) * 128], transpose=True)),
                waits=w)
            t_x = dma("sync", (lambda tt=tt, xb=xb: nc.sync.dma_start(
                out=xb, in_=xn_sp[tt * 128:(tt + 1) * 128, :])))
            wv_ = [t_o, t_x]
            if y_t:
                wv_.append(y_t[-1])
            rec("vector", (lambda tt=tt, ob=ob, xb=xb: nc.vector.tensor_add(
                xstg[:, :], xb, ob)),
                waits=wv_, inc="sV")
            out_v[tt] = R.get("sV")
            y_t.append(dma("gpsimd", (lambda tt=tt: nc.gpsimd.dma_start(
                out=y_out[tt * 128:(tt + 1) * 128, :], in_=xstg[:, :])),
                waits=[("sV", out_v[tt])]))
        rec("gpsimd", (lambda: nc.gpsimd.engine_nop()),
            waits=_cover(y_t))

        blk = e(nc.Block())

        def emitter(name):
            def run(eng):
                for waits, fn, inc, inc_n in R.ops[name]:
                    for s, v in waits:
                        eng.wait_ge(sems[s], v)
                    inst = fn()
                    if inc is not None:
                        inst.then_inc(sems[inc], inc_n)
            return run

        blk.sync(emitter("sync"))
        blk.scalar(emitter("scalar"))
        blk.vector(emitter("vector"))
        blk.tensor(emitter("tensor"))
        blk.gpsimd(emitter("gpsimd"))

    return nc


_NC_CACHE = {}


def kernel(L0_attn_x, L0_attn_Wq, L0_attn_Wk, L0_attn_Wv, L0_attn_Wo,
           L0_ffn_W1, L0_ffn_W2, L0_ffn_W3):
    x = np.ascontiguousarray(np.asarray(L0_attn_x))
    wq = np.ascontiguousarray(np.asarray(L0_attn_Wq))
    wk = np.ascontiguousarray(np.asarray(L0_attn_Wk))
    wv = np.ascontiguousarray(np.asarray(L0_attn_Wv))
    wo = np.ascontiguousarray(np.asarray(L0_attn_Wo))
    w1 = np.ascontiguousarray(np.asarray(L0_ffn_W1))
    w2 = np.ascontiguousarray(np.asarray(L0_ffn_W2))
    w3 = np.ascontiguousarray(np.asarray(L0_ffn_W3))

    if "nc" not in _NC_CACHE:
        _NC_CACHE["nc"] = build_nc()
    nc = _NC_CACHE["nc"]

    in_maps = []
    for c in range(8):
        b, hf = c // 2, c % 2
        in_maps.append({
            "x_in": np.ascontiguousarray(x[b, hf * T:(hf + 1) * T, :]),
            "xp_in": np.ascontiguousarray(x[b, (1 - hf) * T:(2 - hf) * T, :]),
            "wq_in": wq, "wk_in": wk, "wv_in": wv, "wo_in": wo,
            "w1s_in": np.ascontiguousarray(w1[c * 1024:(c + 1) * 1024, :]),
            "w3s_in": np.ascontiguousarray(w3[c * 1024:(c + 1) * 1024, :]),
            "w2s_in": np.ascontiguousarray(w2[c * 256:(c + 1) * 256, :]),
        })
    res = run_bass_kernel_spmd(nc, in_maps, core_ids=list(range(8)))
    out = np.empty((4, 4096, 2048), np.float32)
    for c in range(8):
        b, hf = c // 2, c % 2
        out[b, hf * T:(hf + 1) * T, :] = res.results[c]["y_out"]
    return out


# version 18
# speedup vs baseline: 1.0049x; 1.0049x over previous
import numpy as np
from contextlib import ExitStack

import concourse.bass as bass
import concourse.mybir as mybir
from concourse.bass_utils import run_bass_kernel_spmd

F32 = mybir.dt.float32
BF16 = mybir.dt.bfloat16
AF = mybir.ActivationFunctionType
ALU = mybir.AluOpType
AX = mybir.AxisListType

D = 2048
T = 2048
H = 8192
NTILE = 16
HTILE = 64
TG = 1024
NGRP = T // TG
MMW = 512
EPS = 1e-5
INV_D = 1.0 / D
SCL_G = 0.125
SCL_W2 = float(1.0 / np.sqrt(float(H)))

AG_W1 = 0
AG_W3 = 16384
AG_W2 = 32768
AG_ROWS = 49152


def _hadamard(n):
    m = np.array([[1.0]], np.float32)
    while m.shape[0] < n:
        m = np.block([[m, m], [m, -m]]).astype(np.float32)
    return m


class _Rec:

    NQ = 4

    def __init__(self):
        self.ops = {"sync": [], "scalar": [], "vector": [], "tensor": [],
                    "gpsimd": []}
        self.cnt = {}
        self.fam = {"sync": [f"qL{i}" for i in range(self.NQ)],
                    "scalar": [f"qS{i}" for i in range(self.NQ)],
                    "gpsimd": [f"qG{i}" for i in range(self.NQ)]}
        self.rot = {"sync": 0, "scalar": 0, "gpsimd": 0}

    def rec(self, engine, fn, waits=(), inc=None, inc_n=1):
        wl = []
        seen = {}
        for s, v in waits:
            assert v <= self.cnt.get(s, 0), \
                f"wait {s}>={v} not yet producible ({self.cnt.get(s, 0)})"
            if s not in seen or seen[s] < v:
                seen[s] = v
        wl = sorted(seen.items())
        if inc is not None:
            self.cnt[inc] = self.cnt.get(inc, 0) + inc_n
        self.ops[engine].append((tuple(wl), fn, inc, inc_n))

    def dma(self, engine, fn, waits=()):
        fam = self.fam[engine]
        sem = fam[self.rot[engine] % self.NQ]
        self.rot[engine] += 1
        prev = self.cnt.get(sem, 0)
        self.rec(engine, fn, waits=list(waits) + [(sem, prev)])
        self.cnt[sem] = prev + 16
        w, f, _, _ = self.ops[engine].pop()
        self.ops[engine].append((w, f, sem, 16))
        return (sem, prev + 16)

    def fence(self, engine):
        return [(s, self.cnt[s]) for s in self.fam[engine] if s in self.cnt]

    def get(self, sem):
        return self.cnt.get(sem, 0)


def _cover(tokens):
    best = {}
    for s, v in tokens:
        if best.get(s, -1) < v:
            best[s] = v
    return sorted(best.items())


def build_nc():
    nc = bass.Bass("TRN2", target_bir_lowering=False)

    x_in = nc.dram_tensor("x_in", [T, D], F32, kind="ExternalInput")
    wq_in = nc.dram_tensor("wq_in", [D, D], F32, kind="ExternalInput")
    wk_in = nc.dram_tensor("wk_in", [D, D], F32, kind="ExternalInput")
    wv_in = nc.dram_tensor("wv_in", [D, D], F32, kind="ExternalInput")
    wo_in = nc.dram_tensor("wo_in", [D, D], F32, kind="ExternalInput")
    w1s_in = nc.dram_tensor("w1s_in", [1024, D], F32, kind="ExternalInput")
    w3s_in = nc.dram_tensor("w3s_in", [1024, D], F32, kind="ExternalInput")
    w2s_in = nc.dram_tensor("w2s_in", [256, H], F32, kind="ExternalInput")
    y_out = nc.dram_tensor("y_out", [T, D], F32, kind="ExternalOutput")

    n_sp = nc.dram_tensor("n_sp", [T, D], BF16)
    vt_sp = nc.dram_tensor("vt_sp", [NTILE, D, 128], BF16)
    xn_sp = nc.dram_tensor("xn_sp", [T, D], BF16)
    m_sp = nc.dram_tensor("m_sp", [T, D], BF16)
    ot_sp = nc.dram_tensor("ot_sp", [D, T], BF16)
    w1bf_sp = nc.dram_tensor("w1bf_sp", [1024, D], BF16)
    w3bf_sp = nc.dram_tensor("w3bf_sp", [1024, D], BF16)
    w2bf_sp = nc.dram_tensor("w2bf_sp", [256, H], BF16)
    gg_i = nc.dram_tensor("gg_i", [D, D], BF16)
    gg_o = nc.dram_tensor("gg_o", [2 * D, D], BF16)
    ag_i = nc.dram_tensor("ag_i", [AG_ROWS, 128], BF16)
    ag_o = nc.dram_tensor("ag_o", [8 * AG_ROWS, 128], BF16,
                          addr_space="Shared")

    R = _Rec()
    rec = R.rec
    dma = R.dma

    with ExitStack() as ctx:
        e = ctx.enter_context
        bufA = e(nc.sbuf_tensor("bufA", [128, 32 * 1024], BF16))
        bufB = e(nc.sbuf_tensor("bufB", [128, 32 * 1024], BF16))
        bufC = e(nc.sbuf_tensor("bufC", [128, 16 * 1024], BF16))
        w13p = [e(nc.sbuf_tensor("w13p0", [128, 4096], BF16)),
                e(nc.sbuf_tensor("w13p1", [128, 4096], BF16))]
        xstg = e(nc.sbuf_tensor("xstg", [128, D], F32))
        fstg = e(nc.sbuf_tensor("fstg", [128, D], F32))
        wstg = e(nc.sbuf_tensor("wstg", [128, D], F32))
        bstg = e(nc.sbuf_tensor("bstg", [128, D], BF16))
        stat = e(nc.sbuf_tensor("stat", [128, 160], F32))
        psumA = e(nc.psum_tensor("psumA", [128, D], F32))
        psumB = e(nc.psum_tensor("psumB", [128, D], F32))

        sems = {}
        for nm in (R.fam["sync"] + R.fam["scalar"] + R.fam["gpsimd"]
                   + ["c_ar", "c_ag", "c_gg", "sT", "sV", "sS"]):
            sems[nm] = e(nc.semaphore(nm))

        PS = (psumA, psumB)

        def tile2k(buf, t):
            return buf[:, t * D:(t + 1) * D]

        def hid_tile(a):
            buf = bufA if a < 32 else bufB
            return buf[:, (a % 32) * TG:((a % 32) + 1) * TG]

        def ln_stats(x_ap, sb, waits_extra, sq_out=None):
            b24 = sb * 24
            for c in range(4):
                rec("vector", (lambda c=c, b24=b24: nc.vector.bn_stats(
                    stat[:, b24 + 6 * c:b24 + 6 * (c + 1)],
                    x_ap[:, 512 * c:512 * (c + 1)])),
                    waits=(waits_extra if c == 0 else []), inc="sV")
            rec("vector", (lambda b24=b24, sb=sb: nc.vector.bn_aggr(
                stat[:, 96 + 2 * sb:98 + 2 * sb], stat[:, b24:b24 + 24])),
                waits=[("sV", R.get("sV"))], inc="sV")
            v_var = R.get("sV")
            rec("scalar", (lambda sb=sb: nc.scalar.activation(
                stat[:, 97 + 2 * sb:98 + 2 * sb],
                stat[:, 97 + 2 * sb:98 + 2 * sb], AF.Ln,
                bias=stat[:, 105:106])),
                waits=[("sV", v_var)], inc="sS")
            rec("scalar", (lambda sb=sb: nc.scalar.activation(
                stat[:, 97 + 2 * sb:98 + 2 * sb],
                stat[:, 97 + 2 * sb:98 + 2 * sb], AF.Exp,
                scale=stat[:, 106:107])),
                waits=[("sS", R.get("sS"))], inc="sS")
            return R.get("sS")

        def ln_norm(x_ap, nrm_out, sb, v_rstd, store_fn, extra_w=()):
            w = [("sS", v_rstd)] + list(extra_w)
            rec("vector", (lambda sb=sb: nc.vector.tensor_scalar(
                nrm_out, x_ap, stat[:, 96 + 2 * sb:97 + 2 * sb],
                stat[:, 97 + 2 * sb:98 + 2 * sb],
                op0=ALU.subtract, op1=ALU.mult)), waits=w, inc="sV")
            nv = R.get("sV")
            tok = None
            if store_fn is not None:
                tok = dma("gpsimd", store_fn, waits=[("sV", nv)])
            return nv, tok

        rec("vector", lambda: nc.vector.memset(stat[:, 105:106], EPS), inc="sV")
        rec("vector", lambda: nc.vector.memset(stat[:, 106:107], -0.5), inc="sV")
        rec("vector", lambda: nc.vector.memset(stat[:, 104:105], SCL_W2),
            inc="sV")

        ln_norm_v = {}
        n_spill_t = []

        w2buf = [bufC[:, 0:H], bufC[:, H:2 * H]]
        pong = bufB[:, 0:H]
        w2_cast_t = []
        for tl in range(2):
            w2_cast_t.append(dma("gpsimd", (lambda tl=tl: nc.gpsimd.dma_start(
                out=w2buf[tl], in_=w2s_in[tl * 128:(tl + 1) * 128, :]))))


        def ln_pipe(items, norm_map, spill, init_gates=()):
            pend = {}
            pend_done = {}
            for k in range(len(items) + 1):
                if k < len(items):
                    src_t, i, dstt, gx = items[k]
                    xb = (xstg, wstg, fstg)[k % 3]
                    w = []
                    if k >= 3:
                        w.append(("sV", pend_done[k - 3]))
                    elif init_gates:
                        w.append(init_gates[k])
                    t_x = dma("sync", (lambda src_t=src_t, i=i, xb=xb:
                        nc.sync.dma_start(
                            out=xb[:, :],
                            in_=src_t[i * 128:(i + 1) * 128, :])), waits=w)
                    sb = k % 3
                    rstd = ln_stats(xb[:, :], sb, [t_x])
                    pend[k] = (i, xb, sb, rstd, dstt, gx)
                j = k - 1
                if j >= 0:
                    i, xb, sb, rstd, dstt, gx = pend.pop(j)
                    if spill:
                        sf = (lambda i=i, dstt=dstt: nc.gpsimd.dma_start(
                            out=n_sp[i * 128:(i + 1) * 128, :], in_=dstt))
                    else:
                        sf = None
                    nv, tok = ln_norm(xb[:, :], dstt, sb, rstd, sf, gx)
                    norm_map[i] = nv
                    pend_done[j] = nv
                    if spill:
                        n_spill_t.append(tok)

        ln_pipe([(x_in, i, tile2k(bufA, i), []) for i in range(NTILE)],
                ln_norm_v, True)
        ln1_own_v = ln_norm_v[NTILE - 1]

        prep_bf_t = {}
        for name, src_, dst in (("w1", w1s_in, w1bf_sp), ("w3", w3s_in, w3bf_sp)):
            toks = []
            prev_store = None
            for i in range(4):
                w = [prev_store] if prev_store is not None else []
                t_l = dma("gpsimd", (lambda src_=src_, i=i: nc.gpsimd.dma_start(
                    out=w13p[0][:, 0:4096].rearrange("p (c j) -> p c j", c=2),
                    in_=src_.ap().rearrange("(c p) e -> p c e", p=128)
                        [:, 2 * i:2 * i + 2, :])), waits=w)
                prev_store = dma("gpsimd", (lambda dst=dst, i=i: nc.gpsimd.dma_start(
                    out=dst[i * 256:(i + 1) * 256, :].rearrange(
                        "(c p) e -> p c e", p=128),
                    in_=w13p[0][:, 0:4096].rearrange("p (c j) -> p c j", c=2))),
                    waits=[t_l])
                toks.append(prev_store)
            prep_bf_t[name] = toks

        bf_tasks = []
        w2_done_v = [None, None]

        def mk_bf(tl, half, op, g, bs, cur, oth, wv_):
            def emit():
                def bf():
                    vc = cur.rearrange("p (g two b) -> p g two b",
                                       g=g, two=2, b=bs)
                    vo = oth.rearrange("p (g two b) -> p g two b",
                                       g=g, two=2, b=bs)
                    return nc.vector.tensor_tensor(
                        vo[:, :, half, :], vc[:, :, 0, :], vc[:, :, 1, :], op)
                rec("vector", bf, waits=wv_, inc="sV")
            return emit

        for tl in range(2):
            cur, oth = w2buf[tl], pong
            first = True
            dists = [32 * 128, 16 * 128, 8 * 128, 4 * 128, 2 * 128, 128,
                     64, 32, 16, 8, 4, 2, 1]
            for bs in dists:
                g_ = H // (2 * bs)
                wv_ = [w2_cast_t[tl]] if first else []
                first = False
                bf_tasks.append(mk_bf(tl, 0, ALU.add, g_, bs, cur, oth, wv_))
                bf_tasks.append(mk_bf(tl, 1, ALU.subtract, g_, bs, cur, oth, []))
                cur, oth = oth, cur
            assert cur is pong

            def mk_copy(tl=tl):
                def emit():
                    rec("vector", (lambda tl=tl: nc.vector.tensor_copy(
                        w2buf[tl], pong)), inc="sV")
                    w2_done_v[tl] = R.get("sV")
                return emit
            bf_tasks.append(mk_copy())

        agin_t = []
        prepT13 = []

        prepT_n = [0]

        def mk_prepT13(name, bfsp, base, dtile):
            def emit():
                k = prepT_n[0]; prepT_n[0] += 1
                stg = w13p[1][:, (k % 2) * 1024:(k % 2) * 1024 + 1024]
                w = _cover(prep_bf_t[name]) if dtile == 0 else []
                if len(agin_t) >= 2:
                    w = _cover([agin_t[-2]]) + w
                t_l = dma("sync", (lambda stg=stg: nc.sync.dma_start(
                    out=stg,
                    in_=bfsp[0:1024, dtile * 128:(dtile + 1) * 128],
                    transpose=True)), waits=w)
                agin_t.append(dma("gpsimd", (lambda stg=stg: nc.gpsimd.dma_start(
                    out=ag_i.ap().rearrange("(ht r) c -> ht r c", r=2048)
                        [base // 2048:base // 2048 + 8,
                         dtile * 128:(dtile + 1) * 128, :].rearrange(
                        "ht p c -> p ht c"),
                    in_=stg.rearrange("p (ht c) -> p ht c", ht=8))),
                    waits=[t_l]))
            return emit

        for name, bfsp, base in (("w1", w1bf_sp, AG_W1), ("w3", w3bf_sp, AG_W3)):
            for dtile in range(NTILE):
                prepT13.append(mk_prepT13(name, bfsp, base, dtile))

        g_evac_v = {}
        g_store_t = []

        bq = list(bf_tasks)
        nt_t = []

        def nt_load(r, extra=()):
            nt_t.append(dma("sync", (lambda r=r: nc.sync.dma_start(
                out=bufB[:, r * D:(r + 1) * D],
                in_=n_sp[0:T, r * 128:(r + 1) * 128], transpose=True)),
                waits=list(extra)))

        for dt in range(NTILE):
            if dt >= 2:
                for _ in range(7):
                    if bq:
                        bq.pop(0)()
            if dt == 10:
                while bq:
                    bq.pop(0)()
            if 1 <= dt <= 6:
                r = 2 + 2 * dt
                nt_load(r, _cover(n_spill_t) if dt == 1 else ())
                nt_load(r + 1)
            if dt == 11:
                for r in range(4):
                    nt_load(r, [("sV", w2_done_v[1])])
            if dt >= 8:
                for _ in range(4):
                    if prepT13:
                        prepT13.pop(0)()
            ps = PS[dt % 2]
            for tc in range(NTILE):
                w0 = []
                if dt == 0:
                    w0.append(("sV", ln_norm_v[tc]))
                if dt >= 2 and tc == 0:
                    w0.append(("sV", g_evac_v[dt - 2]))
                for q in range(4):
                    is_last = (tc == NTILE - 1 and q == 3)
                    rec("tensor", (lambda dt=dt, tc=tc, q=q, ps=ps:
                        nc.tensor.matmul(
                            ps[:, q * MMW:(q + 1) * MMW],
                            tile2k(bufA, tc)[:, dt * 128:(dt + 1) * 128],
                            tile2k(bufA, tc)[:, q * MMW:(q + 1) * MMW],
                            start=(tc == 0), stop=(tc == NTILE - 1))),
                        waits=(w0 if q == 0 else []),
                        inc=("sT" if is_last else None))
            vt_ = R.get("sT")
            w = [("sT", vt_)]
            if g_store_t:
                w.append(g_store_t[-1])
            rec("vector", (lambda ps=ps: nc.vector.tensor_scalar_mul(
                bstg[:, :], ps[:, :], SCL_G)), waits=w, inc="sV")
            g_evac_v[dt] = R.get("sV")
            g_store_t.append(dma("sync", (lambda dt=dt: nc.sync.dma_start(
                out=gg_i[dt * 128:(dt + 1) * 128, :], in_=bstg[:, :])),
                waits=[("sV", g_evac_v[dt])]))
        assert not bq
        g_own_last_grp = R.get("sT")

        w2sp_t = []
        for tl in range(2):
            w2sp_t.append(dma("gpsimd", (lambda tl=tl: nc.gpsimd.dma_start(
                out=w2bf_sp[tl * 128:(tl + 1) * 128, :], in_=w2buf[tl])),
                waits=[("sV", w2_done_v[tl])]))
        prep_bf_t["w2"] = w2sp_t

        nt_cov = _cover(nt_t)

        def nt_src(dc):
            return bufB[:, dc * D:(dc + 1) * D]

        def mk_prepT2(hc):
            def emit():
                stg = w13p[1][:, (hc % 2) * 1024:(hc % 2) * 1024 + 256]
                w = _cover(prep_bf_t["w2"]) if hc == 0 else []
                if len(agin_t) >= 2:
                    w += _cover([agin_t[-2]])
                t_l = dma("sync", (lambda stg=stg: nc.sync.dma_start(
                    out=stg,
                    in_=w2bf_sp[0:256, hc * 128:(hc + 1) * 128],
                    transpose=True)), waits=w)
                agin_t.append(dma("gpsimd", (lambda stg=stg: nc.gpsimd.dma_start(
                    out=ag_i.ap().rearrange("(gp r) c -> gp r c", r=8192)
                        [AG_W2 // 8192:AG_W2 // 8192 + 2,
                         hc * 128:(hc + 1) * 128, :].rearrange(
                        "gp p c -> p gp c"),
                    in_=stg.rearrange("p (gp c) -> p gp c", gp=2))),
                    waits=[t_l]))
            return emit

        prepT2 = [mk_prepT2(hc) for hc in range(HTILE)]
        assert not prepT13

        rec("gpsimd", (lambda: nc.gpsimd.collective_compute(
            "AllGather", ALU.bypass,
            replica_groups=[[0, 1], [2, 3], [4, 5], [6, 7]],
            ins=[gg_i.ap().opt()], outs=[gg_o.ap().opt()])),
            waits=_cover(g_store_t), inc="c_gg")
        gg_done = R.get("c_gg")

        vt_grp_t = {}
        vt_evac_v = {}
        vt_store_t = []
        wcp_s = {}

        def wstage_load(k, in_fn, waits_extra):
            stg = (xstg, wstg)[k % 2]
            w = list(waits_extra)
            if k >= 2:
                w.append(("sS", wcp_s[k - 2]))
            return dma("sync", (lambda stg=stg, in_fn=in_fn:
                nc.sync.dma_start(
                    out=stg[:, :].rearrange("p (c j) -> p c j", c=16),
                    in_=in_fn())), waits=w), stg

        def wstage_copy(k, t_l, stg, dst, waits_extra):
            rec("scalar", (lambda dst=dst, stg=stg: nc.scalar.copy(
                dst, stg[:, :])), waits=[t_l] + list(waits_extra), inc="sS")
            wcp_s[k] = R.get("sS")
            return wcp_s[k]

        for ft in range(NTILE + 1):
            for _ in range(4):
                if prepT2:
                    prepT2.pop(0)()
            if ft < NTILE:
                pan = w13p[0][:, (ft % 2) * D:((ft % 2) + 1) * D]
                wl = []
                if ft < 2:
                    wl.append(("sV", ln_norm_v[(15, 13)[ft]]))
                t_l, stg = wstage_load(ft, (lambda ft=ft:
                    wv_in.ap().rearrange("(c p) e -> p c e", p=128)
                        [:, :, ft * 128:(ft + 1) * 128]), wl)
                wc = []
                if ft >= 2:
                    wc.append(("sT", vt_grp_t[ft - 2]))
                if ft == 0:
                    wc += _cover(prep_bf_t["w1"] + prep_bf_t["w3"])
                wstage_copy(ft, t_l, stg, pan, wc)
            if ft >= 1:
                f = ft - 1
                pan = w13p[0][:, (f % 2) * D:((f % 2) + 1) * D]
                ps = PS[f % 2]
                w0 = [("sS", wcp_s[f])] + nt_cov
                if f == 0:
                    w0.append(("sV", g_evac_v[NTILE - 2]))
                elif f == 1:
                    w0.append(("sV", g_evac_v[NTILE - 1]))
                else:
                    w0.append(("sV", vt_evac_v[f - 2]))
                for dc in range(NTILE):
                    for q in range(4):
                        is_last = (dc == NTILE - 1 and q == 3)
                        rec("tensor", (lambda dc=dc, q=q, ps=ps, pan=pan:
                            nc.tensor.matmul(
                                ps[:, q * MMW:(q + 1) * MMW],
                                pan[:, dc * 128:(dc + 1) * 128],
                                nt_src(dc)[:, q * MMW:(q + 1) * MMW],
                                start=(dc == 0), stop=(dc == NTILE - 1))),
                            waits=(w0 if (dc == 0 and q == 0) else []),
                            inc=("sT" if is_last else None))
                vt_grp_t[f] = R.get("sT")
                w = [("sT", vt_grp_t[f])]
                if vt_store_t:
                    w.append(vt_store_t[-1])
                else:
                    w.append(g_store_t[-1])
                rec("vector", (lambda ps=ps: nc.vector.tensor_copy(
                    bstg[:, :], ps[:, :])), waits=w, inc="sV")
                vt_evac_v[f] = R.get("sV")
                vt_store_t.append(dma("gpsimd", (lambda f=f: nc.gpsimd.dma_start(
                    out=vt_sp.ap()[:, f * 128:(f + 1) * 128, :].rearrange(
                        "tt p j -> p tt j"),
                    in_=bstg[:, :].rearrange("p (tt j) -> p tt j", tt=16))),
                    waits=[("sV", vt_evac_v[f])]))
        vt_all_evac_v = vt_evac_v[NTILE - 1]
        assert not prepT2
        rec("gpsimd", (lambda: nc.gpsimd.collective_compute(
            "AllGather", ALU.bypass,
            replica_groups=[[0, 1, 2, 3, 4, 5, 6, 7]],
            ins=[ag_i.ap().opt()], outs=[ag_o.ap().opt()])),
            waits=_cover(agin_t), inc="c_ag")
        ag_done = R.get("c_ag")

        gadd_v = {}
        for dt in range(NTILE):
            stgA = bufC[:, (dt % 2) * 4096:(dt % 2) * 4096 + 2048]
            stgB = bufC[:, (dt % 2) * 4096 + 2048:(dt % 2) * 4096 + 4096]
            w = []
            if dt == 0:
                w = [("c_gg", gg_done)] + _cover(w2sp_t)
            if dt >= 2:
                w = [("sV", gadd_v[dt - 2])]
            tA = dma("sync", (lambda dt=dt, stgA=stgA: nc.sync.dma_start(
                out=stgA, in_=gg_o[dt * 128:(dt + 1) * 128, :])), waits=w)
            tB = dma("sync", (lambda dt=dt, stgB=stgB: nc.sync.dma_start(
                out=stgB, in_=gg_o[D + dt * 128:D + (dt + 1) * 128, :])))
            wa = [tA, tB]
            if dt == 0:
                wa.append(("sT", g_own_last_grp))
            rec("vector", (lambda dt=dt, stgA=stgA, stgB=stgB:
                nc.vector.tensor_add(tile2k(bufA, dt), stgA, stgB)),
                waits=wa, inc="sV")
            gadd_v[dt] = R.get("sV")
        g_asm_v = gadd_v[NTILE - 1]

        t2_evac_s = {}
        t2_qgrp = {}
        wk_q_s = {}

        def quarter_unit(kbase, q, cc, w_in_t, dest_half_gate, out_s):
            k = kbase + 4 * q + cc
            t_l, stg = wstage_load(k, (lambda q=q, cc=cc, w_in_t=w_in_t:
                w_in_t.ap().rearrange("(c p) e -> p c e", p=128)
                    [:, 4 * cc:4 * cc + 4, q * MMW:(q + 1) * MMW]), [])
            dst = bufC[:, (q % 2) * H + cc * 2048:(q % 2) * H + cc * 2048
                       + 2048]
            out_s[q] = wstage_copy(k, t_l, stg, dst, dest_half_gate)

        for cc in range(4):
            quarter_unit(16, 0, cc, wk_in, [("sV", g_asm_v)], wk_q_s)
        kk = 0
        for q in range(4):
            for dt in range(NTILE):
                if q < 3 and dt in (2, 5, 8, 11):
                    g = (_cover(w2sp_t) if q == 0
                         else [("sT", t2_qgrp[q - 1])])
                    quarter_unit(16, q + 1, (dt - 2) // 3, wk_in, g, wk_q_s)
                bank = kk % 8
                ps = PS[bank // 4][:, (bank % 4) * MMW:((bank % 4) + 1) * MMW]
                w0 = [("sS", wk_q_s[q])]
                if kk == 0:
                    w0.append(("sV", g_asm_v))
                    w0.append(("sV", vt_all_evac_v))
                if kk >= 8:
                    w0.append(("sS", t2_evac_s[kk - 8]))
                for dc in range(NTILE):
                    is_last = (dc == NTILE - 1)
                    rec("tensor", (lambda dt=dt, dc=dc, q=q, ps=ps:
                        nc.tensor.matmul(
                            ps,
                            tile2k(bufA, dc)[:, dt * 128:(dt + 1) * 128],
                            bufC[:, (q % 2) * H + dc * MMW:
                                 (q % 2) * H + (dc + 1) * MMW],
                            start=(dc == 0), stop=is_last)),
                        waits=(w0 if dc == 0 else []),
                        inc=("sT" if is_last else None))
                vt_ = R.get("sT")
                rec("scalar", (lambda dt=dt, q=q, ps=ps: nc.scalar.copy(
                    tile2k(bufB, dt)[:, q * MMW:(q + 1) * MMW], ps)),
                    waits=[("sT", vt_)], inc="sS")
                t2_evac_s[kk] = R.get("sS")
                kk += 1
            t2_qgrp[q] = R.get("sT")
        t2_done_s = R.get("sS")

        pq_pan_s = {}
        p_grp_t = {}
        pa_v = {}
        for et in range(NTILE + 1):
            if et < NTILE:
                pa = w13p[et % 2][:, 0:2048]
                t_l, stg = wstage_load(32 + et, (lambda et=et:
                    wq_in.ap().rearrange("(c p) e -> p c e", p=128)
                        [:, :, et * 128:(et + 1) * 128]), [])
                wc = []
                if et >= 2:
                    wc.append(("sT", p_grp_t[et - 2]))
                if et == 0:
                    wc.append(("sT", vt_grp_t[14]))
                if et == 1:
                    wc += _cover(agin_t)
                pq_pan_s[et] = wstage_copy(32 + et, t_l, stg, pa, wc)
            if et >= 1:
                f = et - 1
                pa = w13p[f % 2][:, 0:2048]
                ps = PS[f % 2]
                w0 = [("sS", pq_pan_s[f]), ("sS", t2_done_s)]
                if f >= 2:
                    w0.append(("sV", pa_v[f - 2]))
                for dc in range(NTILE):
                    for q in range(4):
                        is_last = (dc == NTILE - 1 and q == 3)
                        rec("tensor", (lambda dc=dc, q=q, ps=ps, pa=pa:
                            nc.tensor.matmul(
                                ps[:, q * MMW:(q + 1) * MMW],
                                pa[:, dc * 128:(dc + 1) * 128],
                                tile2k(bufB, dc)[:, q * MMW:(q + 1) * MMW],
                                start=(dc == 0), stop=(dc == NTILE - 1))),
                            waits=(w0 if (dc == 0 and q == 0) else []),
                            inc=("sT" if is_last else None))
                p_grp_t[f] = R.get("sT")
                rec("vector", (lambda ps=ps: nc.vector.reduce_max(
                    stat[:, 0:1], ps[:, :], axis=AX.X, negate=True)),
                    waits=[("sT", p_grp_t[f])], inc="sV")
                vm = R.get("sV")
                rec("scalar", (lambda ps=ps: nc.scalar.activation(
                    fstg[:, :], ps[:, :], AF.Exp, bias=stat[:, 0:1],
                    accum_out=stat[:, 1:2])),
                    waits=[("sV", vm)], inc="sS")
                vs = R.get("sS")
                rec("vector", (lambda: nc.vector.reciprocal(
                    stat[:, 2:3], stat[:, 1:2])),
                    waits=[("sS", vs)], inc="sV")
                vr = R.get("sV")
                rec("vector", (lambda f=f: nc.vector.tensor_scalar_mul(
                    tile2k(bufA, f), fstg[:, :], stat[:, 2:3])),
                    waits=[("sV", vr)], inc="sV")
                pa_v[f] = R.get("sV")
        a_done_v = R.get("sV")

        s_evac_s = {}
        s_qgrp = {}
        wo_q_s = {}
        for cc in range(4):
            quarter_unit(48, 0, cc, wo_in, [("sT", t2_qgrp[2])], wo_q_s)
        kk = 0
        for q in range(4):
            for ftl in range(NTILE):
                if q < 3 and ftl in (2, 5, 8, 11):
                    g = ([("sT", t2_qgrp[3])] if q == 0
                         else [("sT", s_qgrp[q - 1])])
                    quarter_unit(48, q + 1, (ftl - 2) // 3, wo_in, g, wo_q_s)
                bank = kk % 8
                ps = PS[bank // 4][:, (bank % 4) * MMW:((bank % 4) + 1) * MMW]
                w0 = [("sS", wo_q_s[q]), ("sV", a_done_v)]
                if kk >= 8:
                    w0.append(("sS", s_evac_s[kk - 8]))
                for ec in range(NTILE):
                    is_last = (ec == NTILE - 1)
                    rec("tensor", (lambda ftl=ftl, ec=ec, q=q, ps=ps:
                        nc.tensor.matmul(
                            ps,
                            tile2k(bufA, ec)[:, ftl * 128:(ftl + 1) * 128],
                            bufC[:, (q % 2) * H + ec * MMW:
                                 (q % 2) * H + (ec + 1) * MMW],
                            start=(ec == 0), stop=is_last)),
                        waits=(w0 if ec == 0 else []),
                        inc=("sT" if is_last else None))
                vt_ = R.get("sT")
                rec("scalar", (lambda ftl=ftl, q=q, ps=ps: nc.scalar.copy(
                    tile2k(bufB, ftl)[:, q * MMW:(q + 1) * MMW], ps)),
                    waits=[("sT", vt_)], inc="sS")
                s_evac_s[kk] = R.get("sS")
                kk += 1
            s_qgrp[q] = R.get("sT")
        s_done_s = R.get("sS")
        s_last_grp = s_qgrp[3]

        fin_v = {}
        fin_grp = {}
        xn_store_t = []
        ln2_norm_v = {}
        ln2_pend = {}
        m_store_t = []
        for tt in range(NTILE):
            vp = bufC[:, (tt % 3) * D:((tt % 3) + 1) * D]
            xb = (xstg, wstg)[tt % 2]
            w = []
            if tt == 0:
                w = _cover(vt_store_t) + [("sS", s_done_s)]
            if tt >= 3:
                w.append(("sT", fin_grp[tt - 3]))
            t_p = dma("sync", (lambda tt=tt, vp=vp: nc.sync.dma_start(
                out=vp.rearrange("p (c j) -> p c j", c=16),
                in_=vt_sp.ap()[tt, :, :].rearrange("(c p) j -> p c j",
                                                   p=128))), waits=w)
            wx = [("sV", fin_v[tt - 2])] if tt >= 2 else \
                [("sS", wcp_s[62 + tt])]
            t_x = dma("sync", (lambda tt=tt, xb=xb: nc.sync.dma_start(
                out=xb[:, :], in_=x_in[tt * 128:(tt + 1) * 128, :])),
                waits=wx)
            ps = PS[tt % 2]
            w0 = [t_p, t_x, ("sS", s_done_s)]
            if tt >= 2:
                w0.append(("sV", fin_v[tt - 2]))
            for fc in range(NTILE):
                for q in range(4):
                    is_last = (fc == NTILE - 1 and q == 3)
                    rec("tensor", (lambda fc=fc, q=q, ps=ps, vp=vp:
                        nc.tensor.matmul(
                            ps[:, q * MMW:(q + 1) * MMW],
                            vp[:, fc * 128:(fc + 1) * 128],
                            tile2k(bufB, fc)[:, q * MMW:(q + 1) * MMW],
                            start=(fc == 0), stop=(fc == NTILE - 1))),
                        waits=(w0 if (fc == 0 and q == 0) else []),
                        inc=("sT" if is_last else None))
            fin_grp[tt] = R.get("sT")
            xb2 = tile2k(bufA, tt % 2)
            ob2 = tile2k(bufA, 2 + tt % 2)
            w = [("sT", fin_grp[tt]), t_x]
            if tt == 0:
                w.append(("sT", s_last_grp))
            if tt >= 2:
                w.append(("sV", ln2_norm_v[tt - 2]))
                w.append(xn_store_t[tt - 2])
            rec("vector", (lambda ps=ps, xb=xb, xb2=xb2: nc.vector.tensor_add(
                xb2, xb[:, :], ps[:, :])), waits=w, inc="sV")
            fin_v[tt] = R.get("sV")
            xn_store_t.append(dma("gpsimd", (lambda tt=tt, xb2=xb2:
                nc.gpsimd.dma_start(
                    out=xn_sp[tt * 128:(tt + 1) * 128, :], in_=xb2)),
                waits=[("sV", fin_v[tt])]))

            sb = tt % 2
            rstd = ln_stats(xb2, sb, [("sV", fin_v[tt])])
            ln2_pend[tt] = (xb2, ob2, sb, rstd)

            def ln2_emit(j):
                xbj, obj, sbj, rstdj = ln2_pend.pop(j)
                ew = [m_store_t[-1]] if m_store_t else []
                nv, tok = ln_norm(
                    xbj, obj, sbj, rstdj,
                    (lambda j=j, obj=obj: nc.gpsimd.dma_start(
                        out=m_sp[j * 128:(j + 1) * 128, :], in_=obj)), ew)
                ln2_norm_v[j] = nv
                m_store_t.append(tok)
            if tt >= 1:
                ln2_emit(tt - 1)
            if tt == NTILE - 1:
                ln2_emit(tt)
        ln2_all_v = ln2_norm_v[NTILE - 1]

        hid_mm_t = {}
        hid_v = {}
        ot_mm_t = {}
        ot_s = {}
        ot_store_t = []
        for g in range(NGRP):
            mt_t = []
            for r in range(NTILE):
                w = []
                if g == 0:
                    if r == 0:
                        w += _cover(m_store_t[:8])
                        w.append(("sT", fin_grp[NTILE - 1]))
                else:
                    if r == 0:
                        w.append(("sT", ot_mm_t[(g - 1, NTILE - 2)]))
                    elif r == 8:
                        w.append(("sT", ot_mm_t[(g - 1, NTILE - 1)]))
                mt_t.append(dma("sync", (lambda g=g, r=r: nc.sync.dma_start(
                    out=bufC[:, r * TG:(r + 1) * TG],
                    in_=m_sp[g * TG + 0:(g + 1) * TG,
                             r * 128:(r + 1) * 128], transpose=True)),
                    waits=w))
            mt_cov = _cover(mt_t)

            for a in range(HTILE):
                rk, loc = a // 8, a % 8
                pa = w13p[a % 2]
                w = []
                if g == 0 and a < 2:
                    w.append(("c_ag", ag_done))
                    w.append(("sV", ln2_all_v))
                    w += _cover(m_store_t)
                if a >= 2:
                    w.append(("sT", hid_mm_t[(g, a - 2)]))
                base1 = (rk * AG_ROWS + AG_W1 + loc * 2048) // 128
                base3 = (rk * AG_ROWS + AG_W3 + loc * 2048) // 128
                t1 = dma("sync", (lambda base1=base1, pa=pa: nc.sync.dma_start(
                    out=pa[:, 0:2048].rearrange("p (c j) -> p c j", c=16),
                    in_=ag_o.ap().rearrange("(r p) c -> r p c", p=128)
                        [base1:base1 + 16].rearrange("r p c -> p r c"))),
                    waits=w)
                t3 = dma("sync", (lambda base3=base3, pa=pa: nc.sync.dma_start(
                    out=pa[:, 2048:4096].rearrange("p (c j) -> p c j", c=16),
                    in_=ag_o.ap().rearrange("(r p) c -> r p c", p=128)
                        [base3:base3 + 16].rearrange("r p c -> p r c"))))
                ps = PS[a % 2]
                w0 = [t1, t3] + mt_cov
                if g == 0 and a < 2:
                    w0.append(("sV", fin_v[14 + a]))
                elif a < 2:
                    w0.append(("sS", ot_s[(g - 1, 14 + a)]))
                else:
                    w0.append(("sV", hid_v[(g, a - 2)]))
                for dc in range(NTILE):
                    for wsel in range(2):
                        for hhf in range(2):
                            is_last = (dc == NTILE - 1 and wsel == 1
                                       and hhf == 1)
                            rec("tensor", (lambda dc=dc, wsel=wsel, hhf=hhf,
                                           ps=ps, pa=pa:
                                nc.tensor.matmul(
                                    ps[:, wsel * TG + hhf * MMW:
                                       wsel * TG + (hhf + 1) * MMW],
                                    pa[:, wsel * 2048 + dc * 128:
                                       wsel * 2048 + (dc + 1) * 128],
                                    bufC[:, dc * TG + hhf * MMW:
                                         dc * TG + (hhf + 1) * MMW],
                                    start=(dc == 0), stop=(dc == NTILE - 1))),
                                waits=(w0 if (dc == 0 and wsel == 0
                                              and hhf == 0) else []),
                                inc=("sT" if is_last else None))
                hid_mm_t[(g, a)] = R.get("sT")
                rec("scalar", (lambda ps=ps: nc.scalar.activation(
                    fstg[:, 0:TG], ps[:, 0:TG], AF.Silu)),
                    waits=[("sT", hid_mm_t[(g, a)])], inc="sS")
                vs = R.get("sS")
                rec("vector", (lambda a=a, ps=ps: nc.vector.tensor_mul(
                    hid_tile(a), fstg[:, 0:TG], ps[:, TG:2 * TG])),
                    waits=[("sS", vs)], inc="sV")
                hid_v[(g, a)] = R.get("sV")
            hid_all_v = R.get("sV")

            for gt in range(NTILE):
                rk2, loc2 = gt // 2, gt % 2
                base2 = (rk2 * AG_ROWS + AG_W2 + loc2 * 8192) // 128
                pa = bufC[:, (gt % 2) * H:((gt % 2) + 1) * H]
                w = []
                if gt < 2:
                    w.append(("sT", hid_mm_t[(g, HTILE - 1)]))
                else:
                    w.append(("sT", ot_mm_t[(g, gt - 2)]))
                t_l = dma("sync", (lambda base2=base2, pa=pa: nc.sync.dma_start(
                    out=pa.rearrange("p (c j) -> p c j", c=64),
                    in_=ag_o.ap().rearrange("(r p) c -> r p c", p=128)
                        [base2:base2 + 64].rearrange("r p c -> p r c"))),
                    waits=w)
                ps = PS[gt % 2]
                w0 = [t_l, ("sV", hid_all_v)]
                if gt >= 2:
                    w0.append(("sS", ot_s[(g, gt - 2)]))
                for hc in range(HTILE):
                    for hhf in range(2):
                        is_last = (hc == HTILE - 1 and hhf == 1)
                        rec("tensor", (lambda hc=hc, hhf=hhf, ps=ps, pa=pa:
                            nc.tensor.matmul(
                                ps[:, hhf * MMW:(hhf + 1) * MMW],
                                pa[:, hc * 128:(hc + 1) * 128],
                                hid_tile(hc)[:, hhf * MMW:(hhf + 1) * MMW],
                                start=(hc == 0), stop=(hc == HTILE - 1))),
                            waits=(w0 if (hc == 0 and hhf == 0) else []),
                            inc=("sT" if is_last else None))
                ot_mm_t[(g, gt)] = R.get("sT")
                w = [("sT", ot_mm_t[(g, gt)])]
                if ot_store_t:
                    w.append(ot_store_t[-1])
                rec("scalar", (lambda ps=ps: nc.scalar.activation(
                    bstg[:, 0:TG], ps[:, 0:TG], AF.Copy,
                    scale=stat[:, 104:105])), waits=w, inc="sS")
                ot_s[(g, gt)] = R.get("sS")
                ot_store_t.append(dma("gpsimd", (lambda gt=gt, g=g:
                    nc.gpsimd.dma_start(
                        out=ot_sp[gt * 128:(gt + 1) * 128,
                                  g * TG:(g + 1) * TG],
                        in_=bstg[:, 0:TG])),
                    waits=[("sS", ot_s[(g, gt)])]))

        out_v = {}
        y_t = []
        for tt in range(NTILE):
            ob = w13p[tt % 2][:, 0:D]
            xb = w13p[tt % 2][:, D:2 * D]
            w = []
            if tt == 0:
                w += _cover(ot_store_t[:NTILE])
                w.append(("sT", hid_mm_t[(NGRP - 1, HTILE - 1)]))
            elif tt == 8:
                w += _cover(ot_store_t)
            if tt >= 2:
                w.append(("sV", out_v[tt - 2]))
            t_o = dma("sync", (lambda tt=tt, ob=ob: nc.sync.dma_start(
                out=ob,
                in_=ot_sp[0:D, tt * 128:(tt + 1) * 128], transpose=True)),
                waits=w)
            t_x = dma("sync", (lambda tt=tt, xb=xb: nc.sync.dma_start(
                out=xb, in_=xn_sp[tt * 128:(tt + 1) * 128, :])))
            wv_ = [t_o, t_x]
            if y_t:
                wv_.append(y_t[-1])
            rec("vector", (lambda tt=tt, ob=ob, xb=xb: nc.vector.tensor_add(
                xstg[:, :], xb, ob)),
                waits=wv_, inc="sV")
            out_v[tt] = R.get("sV")
            y_t.append(dma("gpsimd", (lambda tt=tt: nc.gpsimd.dma_start(
                out=y_out[tt * 128:(tt + 1) * 128, :], in_=xstg[:, :])),
                waits=[("sV", out_v[tt])]))
        rec("gpsimd", (lambda: nc.gpsimd.engine_nop()),
            waits=_cover(y_t))

        blk = e(nc.Block())

        def emitter(name):
            def run(eng):
                for waits, fn, inc, inc_n in R.ops[name]:
                    for s, v in waits:
                        eng.wait_ge(sems[s], v)
                    inst = fn()
                    if inc is not None:
                        inst.then_inc(sems[inc], inc_n)
            return run

        blk.sync(emitter("sync"))
        blk.scalar(emitter("scalar"))
        blk.vector(emitter("vector"))
        blk.tensor(emitter("tensor"))
        blk.gpsimd(emitter("gpsimd"))

    return nc


_NC_CACHE = {}


def kernel(L0_attn_x, L0_attn_Wq, L0_attn_Wk, L0_attn_Wv, L0_attn_Wo,
           L0_ffn_W1, L0_ffn_W2, L0_ffn_W3):
    x = np.ascontiguousarray(np.asarray(L0_attn_x))
    wq = np.ascontiguousarray(np.asarray(L0_attn_Wq))
    wk = np.ascontiguousarray(np.asarray(L0_attn_Wk))
    wv = np.ascontiguousarray(np.asarray(L0_attn_Wv))
    wo = np.ascontiguousarray(np.asarray(L0_attn_Wo))
    w1 = np.ascontiguousarray(np.asarray(L0_ffn_W1))
    w2 = np.ascontiguousarray(np.asarray(L0_ffn_W2))
    w3 = np.ascontiguousarray(np.asarray(L0_ffn_W3))

    if "nc" not in _NC_CACHE:
        _NC_CACHE["nc"] = build_nc()
    nc = _NC_CACHE["nc"]

    in_maps = []
    for c in range(8):
        b, hf = c // 2, c % 2
        in_maps.append({
            "x_in": np.ascontiguousarray(x[b, hf * T:(hf + 1) * T, :]),
            "wq_in": wq, "wk_in": wk, "wv_in": wv, "wo_in": wo,
            "w1s_in": np.ascontiguousarray(w1[c * 1024:(c + 1) * 1024, :]),
            "w3s_in": np.ascontiguousarray(w3[c * 1024:(c + 1) * 1024, :]),
            "w2s_in": np.ascontiguousarray(w2[c * 256:(c + 1) * 256, :]),
        })
    res = run_bass_kernel_spmd(nc, in_maps, core_ids=list(range(8)))
    out = np.empty((4, 4096, 2048), np.float32)
    for c in range(8):
        b, hf = c // 2, c % 2
        out[b, hf * T:(hf + 1) * T, :] = res.results[c]["y_out"]
    return out
